# version 24
# baseline (speedup 1.0000x reference)
import numpy as np
from contextlib import ExitStack

import concourse.bass as bass
import concourse.mybir as mybir
from concourse.bass_utils import run_bass_kernel_spmd

F32 = mybir.dt.float32
BF16 = mybir.dt.bfloat16
AF = mybir.ActivationFunctionType
ALU = mybir.AluOpType
AX = mybir.AxisListType

PHASE_LIMIT = None
DEBUG_DUMP = False
LAST_PROG = None
P1_SERIAL = False
P8_STEPS = {"load", "mm", "bias", "silu", "sgn", "tree", "msk", "tp", "mte"}
LENIENT = False
P8_PAIRS = 2
NB = 4
L = 2048
D = 1024
LQ = 512
NQ = L // LQ
NM = 8
NDC = 16
NC = 8
NST = 16
NWC = 16
WPAD = 2048
XW = 520
FE_ORDER = list(range(8, 16)) + list(range(0, 8))


class Prog:
    ENG = ("y", "g", "t", "s", "v")

    def __init__(self, nc, ctx):
        self.nc = nc
        self.sems = {}
        self.ctr = {}
        for nm in ("T", "V", "S", "G", "X", "XG", "W", "WG", "O"):
            self.sems[nm] = ctx.enter_context(nc.semaphore("sem" + nm))
            self.ctr[nm] = 0
        self.tok = {}
        self.phases = []
        self.cur = None

    def phase(self, name):
        self.cur = {k: [] for k in self.ENG}
        self.phases.append((name, self.cur))

    def raw(self, eng, fn):
        self.cur[eng].append(("raw", fn))

    def inc(self, eng, ctr, key, fn):
        step = 16 if ctr in ("X", "XG", "W", "WG", "O") else 1
        self.ctr[ctr] += step
        if key is not None:
            assert key not in self.tok, key
            self.tok[key] = (ctr, self.ctr[ctr])
        self.cur[eng].append(("inc", fn, self.sems[ctr], step, ctr, key))

    def wait(self, eng, key, must=True):
        self.cur[eng].append(("wait", key, must))

    def wait_val(self, eng, ctr, val):
        self.cur[eng].append(("waitv", self.sems[ctr], val))

    def _remap_dma_bursts(self):
        for eng in self.ENG:
            pend = {}
            last = {}

            def flush():
                for ctr, keys in pend.items():
                    for k in keys:
                        self.tok[k] = (ctr, last[ctr])
                pend.clear()

            for _, streams in self.phases:
                for it in streams[eng]:
                    if it[0] == "inc" and it[4] in ("X", "XG", "W", "WG", "O"):
                        ctr, key = it[4], it[5]
                        last[ctr] = last.get(ctr, 0) + it[3]
                        if key is not None:
                            pend.setdefault(ctr, []).append(key)
                    elif it[0] == "waitv":
                        flush()
                    elif it[0] == "wait" and it[1] in self.tok:
                        flush()
                flush()

    def realize(self):
        self._remap_dma_bursts()
        nc = self.nc
        for name, streams in self.phases:
            with nc.Block(name) as blk:
                for eng in self.ENG:
                    items = streams[eng]
                    if not items:
                        continue
                    attr = {"y": "sync", "g": "gpsimd", "t": "tensor",
                            "s": "scalar", "v": "vector"}[eng]

                    def body(e, items=items):
                        for it in items:
                            kind = it[0]
                            if kind == "raw":
                                it[1](e)
                            elif kind == "inc":
                                fn, sem, step = it[1], it[2], it[3]
                                fn(e).then_inc(sem, step)
                            elif kind == "wait":
                                _, key, must = it
                                if key in self.tok:
                                    ctr, v = self.tok[key]
                                    e.wait_ge(self.sems[ctr], v)
                                elif must and not LENIENT:
                                    raise KeyError(f"missing token {key}")
                            else:
                                _, sem, v = it
                                e.wait_ge(sem, v)

                    getattr(blk, attr)(body)


def build_nc():
    global LAST_PROG
    nc = bass.Bass("TRN2", target_bir_lowering=False,
                   detect_race_conditions=False)

    x_h = nc.dram_tensor("x", [NB * L, D], F32, kind="ExternalInput")
    out_h = nc.dram_tensor("out", [NB * L, D], F32, kind="ExternalOutput")
    win_h = nc.dram_tensor("win_re", [16 * 128, 1024], F32,
                           kind="ExternalInput")
    wxp_h = nc.dram_tensor("w_xp", [1024, 112], F32, kind="ExternalInput")
    wdt_h = nc.dram_tensor("w_dt", [64, 1024], F32, kind="ExternalInput")
    wout_h = nc.dram_tensor("w_out", [2048, 1024], F32, kind="ExternalInput")
    wfiT_h = nc.dram_tensor("wfiT", [1024, WPAD], F32, kind="ExternalInput")
    wfoT_h = nc.dram_tensor("wfoT", [WPAD, 1024], F32, kind="ExternalInput")
    bin_h = nc.dram_tensor("binp", [1, WPAD], F32, kind="ExternalInput")
    bdt_h = nc.dram_tensor("bdt8", [128, 8], F32, kind="ExternalInput")
    d8_h = nc.dram_tensor("d8", [128, 8], F32, kind="ExternalInput")
    bx_h = nc.dram_tensor("bx8", [128, 8], F32, kind="ExternalInput")
    bz_h = nc.dram_tensor("bz8", [128, 8], F32, kind="ExternalInput")
    kx_h = nc.dram_tensor("kx32", [128, 32], F32, kind="ExternalInput")
    kz_h = nc.dram_tensor("kz32", [128, 32], F32, kind="ExternalInput")
    alog_h = nc.dram_tensor("alog", [128, 128], F32, kind="ExternalInput")
    win16_h = nc.dram_tensor("win16d", [16 * 128, 1024], BF16, kind="Internal")
    wfo16_h = nc.dram_tensor("wfo16d", [WPAD, 1024], BF16, kind="Internal")
    wo16_h = nc.dram_tensor("wo16d", [2048, 1024], BF16, kind="Internal")

    ctx = ExitStack()
    sb = lambda n, s, d=F32: ctx.enter_context(nc.sbuf_tensor(n, s, d))
    with ctx:
        Wfi16r = sb("Wfi16r", [128, 16 * 1024], BF16)
        Winb = sb("Winb", [128, 2 * 1024], BF16)
        Wxp16 = sb("Wxp16", [128, 8 * 112], BF16)
        Wdt16 = sb("Wdt16", [64, 1024], BF16)
        bin16 = sb("bin16", [1, WPAD], BF16)
        ones16 = sb("ones16", [1, 128], BF16)
        ident16 = sb("ident16", [128, 128], BF16)
        bdt8 = sb("bdt8s", [128, 8]); d8 = sb("d8s", [128, 8])
        bx8 = sb("bx8s", [128, 8]); bz8 = sb("bz8s", [128, 8])
        kx32 = sb("kx32s", [128, 32]); kz32 = sb("kz32s", [128, 32])
        a128 = sb("a128", [128, 128])
        hprev = sb("hprev", [128, 128])
        xnT = sb("xnT", [128, 8 * XW], BF16)
        u16 = sb("u16", [128, 2 * 8 * LQ], BF16)
        ycat16 = sb("ycat16", [128, 24 * LQ], BF16)
        B16 = sb("B16", [128, 16 * LQ], BF16)
        C16 = sb("C16", [128, 16 * LQ], BF16)
        tmpall = sb("tmpall", [128, 16 * LQ], BF16)
        mskT16 = sb("mskT16", [128, 16 * LQ], BF16)
        xz16 = sb("xz16", [128, 516 + 8], BF16)
        diag8 = sb("diag8", [128, 512], BF16)
        dtv16 = sb("dtv16", [64, LQ], BF16)
        braw = sb("braw", [16, LQ], BF16)
        craw = sb("craw", [16, LQ], BF16)
        delta2v = sb("delta2v", [128, 2 * LQ])
        du = sb("du", [128, LQ])
        dA2 = sb("dA2", [128, 3 * LQ])
        dBu2 = sb("dBu2", [128, LQ], BF16)
        xt2 = sb("xt2", [128, 2 * D], BF16)
        xt9 = sb("xt9", [128, 2 * D], BF16)
        xn2 = sb("xn2", [128, 2 * D], BF16)
        outt2 = sb("outt2", [128, 2 * D])
        Wo16 = sb("Wo16", [128, 2 * 1024], BF16)
        Wf16 = sb("Wf16", [128, 2 * 1024], BF16)
        lns = sb("lns", [128, 14])
        cpool = xz16[:, 516:524].bitcast(F32)
        c_eps = cpool[:, 0:1]
        c_invd = cpool[:, 1:2]
        nc.const_aps.register(F32, 1.0 / D, c_invd)
        onehot16 = sb("onehot16", [16, 16 * 128], BF16)

        ps = [ctx.enter_context(nc.psum_tensor(f"ps{i}", [128, 512], F32))
              for i in range(8)]

        acts16 = tmpall[:, 0:2048]
        sgn16 = tmpall[:, 2048:4096]
        dmap16 = tmpall[:, 4096:6144]
        msk16 = tmpall[:, 6144:8192]

        pr = Prog(nc, ctx)
        LAST_PROG = pr

        pr.phase("p0")
        for i, (dst, src) in enumerate([(bdt8, bdt_h), (d8, d8_h), (bx8, bx_h),
                                        (bz8, bz_h), (kx32, kx_h), (kz32, kz_h)]):
            pr.inc("y", "X", ("cst", i),
                   lambda e, d=dst, s=src: e.dma_start(d[:], s.ap()))
        pr.inc("y", "X", ("alog",), lambda e: e.dma_start(a128[:], alog_h.ap()))
        for m in range(NM):
            for h in range(2):
                pr.inc("g", "WG", ("wfir", m, h),
                       lambda e, m=m, h=h: e.dma_start(
                           Wfi16r[:, m * 2048 + h * 1024:
                                  m * 2048 + (h + 1) * 1024],
                           wfiT_h.ap()[m * 128:(m + 1) * 128,
                                       h * 1024:(h + 1) * 1024]))
        for c in range(NC):
            pr.inc("g", "WG", ("wxp", c),
                   lambda e, c=c: e.dma_start(
                       Wxp16[:, c * 112:(c + 1) * 112],
                       wxp_h.ap()[c * 128:(c + 1) * 128, :]))
        pr.inc("g", "WG", ("wdt",), lambda e: e.dma_start(Wdt16[:], wdt_h.ap()))
        pr.inc("g", "WG", ("bin",), lambda e: e.dma_start(bin16[:], bin_h.ap()))
        pr.inc("g", "G", ("iota",),
               lambda e: e.iota(xn2[:, 0:128], [[1, 128]],
                                channel_multiplier=-1,
                                allow_small_or_imprecise_dtypes=True))
        pr.inc("g", "G", ("ioh",),
               lambda e: e.iota(tmpall[0:16, 0:2048], [[1, 16], [0, 128]],
                                channel_multiplier=-1,
                                allow_small_or_imprecise_dtypes=True))
        pr.wait("s", ("alog",))
        pr.inc("s", "S", ("aexp",),
               lambda e: e.activation(outt2[:, 0:128], a128[:], AF.Exp))
        pr.wait("v", ("aexp",))
        pr.raw("v", lambda e: e.tensor_scalar_mul(a128[:], outt2[:, 0:128],
                                                  -1.0))
        pr.raw("v", lambda e: e.memset(c_eps, 1e-5))
        pr.raw("v", lambda e: e.memset(c_invd, 1.0 / D))
        pr.raw("v", lambda e: e.memset(xnT[:], 0.0))
        pr.raw("v", lambda e: e.memset(tmpall[:, 4096 + 2040:4096 + 2048],
                                       0.0))
        pr.wait("v", ("iota",))
        pr.raw("v", lambda e: e.tensor_single_scalar(ident16[:], xn2[:, 0:128],
                                                     0.0, ALU.is_equal))
        pr.raw("v", lambda e: e.memset(ones16[:], 1.0))
        pr.wait("v", ("ioh",))
        pr.raw("v", lambda e: e.tensor_single_scalar(
            onehot16[:], tmpall[0:16, 0:2048], 0.0, ALU.is_equal))

        last_out = [None, None]
        it_ctr = [0]

        pr.phase("main")
        pieces = [(b, q) for b in range(NB) for q in range(NQ)]

        def emit_P9(pk, pp, nxt=None):
            b, q = pk
            l0 = q * LQ
            def emit_x9(t):
                r0 = b * L + l0 + t * 128
                pr.wait("g", ("res", b, q, t - 2), must=False)
                if t < 2 and pp is not None:
                    pr.wait("g", ("res",) + pp + (2 + t,), must=False)
                pr.inc("g", "XG", ("x9", b, q, t),
                       lambda e, t=t, r0=r0: e.dma_start(
                           xt9[:, (t % 2) * D:(t % 2 + 1) * D],
                           x_h.ap()[r0:r0 + 128, :]))
            emit_x9(0)
            emit_x9(1)
            dcseq = list(range(8, 16)) + list(range(0, 8))
            for i, dc in enumerate(dcseq):
                if (b, q) == (0, 0) and i == 0:
                    pr.wait("y", ("wcast", 2))
                if i < 2 and pp is not None:
                    pr.wait("y", ("mmo",) + pp + (6 + i,), must=False)
                if i >= 2:
                    pr.wait("y", ("mmo", b, q, dcseq[i - 2]))
                pr.inc("y", "W", ("wo", b, q, dc),
                       lambda e, dc=dc: e.dma_start(
                           Wo16[:, (dc % 2) * 1024:(dc % 2 + 1) * 1024],
                           wo16_h.ap()[dc * 128:(dc + 1) * 128, :]))
            for wc in range(NWC):
                if wc == 0 and pp is not None:
                    pr.wait("y", ("mmf",) + pp + (14,), must=False)
                if wc == 1 and pp is not None:
                    pr.wait("y", ("odone",) + pp + (3,), must=False)
                pr.wait("y", ("mmf", b, q, wc - 2), must=False)
                pr.inc("y", "W", ("wf", b, q, wc),
                       lambda e, wc=wc: e.dma_start(
                           Wf16[:, (wc % 2) * 1024:(wc % 2 + 1) * 1024],
                           wfo16_h.ap()[wc * 128:(wc + 1) * 128, :]))

            for i, dc in enumerate(dcseq):
                pr.wait("t", ("wo", b, q, dc))
                if i == 0:
                    pr.wait("t", ("silu", b, q, 3))
                    pr.wait("t", ("mte", b, q, 3))
                    pr.wait("t", ("gelu", b, q, 15))
                    pr.wait("t", ("delta", b, q, 7))
                    pr.wait("t", ("Cev", b, q, 15))
                    if nxt is not None:
                        pr.wait("t", ("xze",) + nxt + (7,))
                        pr.wait("t", ("xde",) + nxt)
                if dc < 8:
                    pr.wait("t", ("ycat", b, q, dc))
                cnt_mm = 0
                for t in range(4):
                    for jh in range(2):
                        cnt_mm += 1
                        key = ("mmo", b, q, dc) if cnt_mm == 8 else None
                        kpar = (b * NQ + q) % 2
                        yoff = (dc * LQ if dc < 8 else
                                (8 + kpar * 8 + dc - 8) * LQ)
                        fn = (lambda e, dc=dc, t=t, jh=jh, yoff=yoff,
                              i=i: e.matmul(
                            ps[t * 2 + jh][:],
                            ycat16[:, yoff + t * 128:yoff + (t + 1) * 128],
                            Wo16[:, (dc % 2) * 1024 + jh * 512:
                                 (dc % 2) * 1024 + (jh + 1) * 512],
                            start=(i == 0), stop=False))
                        (pr.inc("t", "T", key, fn) if key
                         else pr.raw("t", fn))
            for wc in range(NWC):
                pr.wait("t", ("wf", b, q, wc))
                cnt_mm = 0
                for t in range(4):
                    for jh in range(2):
                        cnt_mm += 1
                        last_wc = (wc == NWC - 1)
                        if last_wc and jh == 1:
                            key = ("odone", b, q, t)
                        elif (not last_wc) and cnt_mm == 8:
                            key = ("mmf", b, q, wc)
                        else:
                            key = None
                        fn = (lambda e, wc=wc, t=t, jh=jh: e.matmul(
                            ps[t * 2 + jh][:],
                            mskT16[:, wc * LQ + t * 128:
                                   wc * LQ + (t + 1) * 128],
                            Wf16[:, (wc % 2) * 1024 + jh * 512:
                                 (wc % 2) * 1024 + (jh + 1) * 512],
                            start=False, stop=(wc == NWC - 1)))
                        (pr.inc("t", "T", key, fn) if key
                         else pr.raw("t", fn))
            for t in range(4):
                par = t % 2
                pr.wait("s", ("odone", b, q, t))
                if last_out[par] is not None:
                    pr.wait("s", last_out[par])
                pr.raw("s", lambda e, t=t, par=par: e.activation(
                    outt2[:, par * D:par * D + 512], ps[t * 2][:],
                    AF.Copy))
                pr.inc("s", "S", ("oev", b, q, t),
                       lambda e, t=t, par=par: e.activation(
                           outt2[:, par * D + 512:(par + 1) * D],
                           ps[t * 2 + 1][:], AF.Copy))
                pr.wait("v", ("oev", b, q, t))
                pr.wait("v", ("x9", b, q, t))
                pr.inc("v", "V", ("res", b, q, t),
                       lambda e, t=t, par=par: e.tensor_tensor(
                           outt2[:, par * D:(par + 1) * D],
                           outt2[:, par * D:(par + 1) * D],
                           xt9[:, par * D:(par + 1) * D], ALU.add))
                if t + 2 < 4:
                    emit_x9(t + 2)
                r0 = b * L + l0 + t * 128
                pr.wait("y", ("res", b, q, t))
                pr.inc("y", "O", ("out", b, q, t),
                       lambda e, par=par, r0=r0: e.dma_start(
                           out_h.ap()[r0:r0 + 128, :],
                           outt2[:, par * D:(par + 1) * D]))
                last_out[par] = ("out", b, q, t)

        pieceit = {}

        def emit_P1(pc, pq):
            b, q = pc
            l0 = q * LQ
            g_lo, g_hi = max(q * 4 - 1, 0), min(q * 4 + 4, 15)
            for g in range(g_lo, g_hi + 1):
                it = it_ctr[0]; it_ctr[0] += 1
                par = it % 2
                lp = par * 7
                r0 = b * L + g * 128
                xt = xt2[:, par * D:(par + 1) * D]
                pr.wait("g", ("xn", it - 2), must=False)
                pr.wait("g", ("ss", it - 2), must=False)
                pr.inc("g", "XG", ("x", it),
                       lambda e, xt=xt, r0=r0: e.dma_start(
                           xt, x_h.ap()[r0:r0 + 128, :]))
                if last_out[par] is not None:
                    pr.wait("s", last_out[par])
                pr.wait("s", ("x", it))
                pr.inc("s", "S", ("ss", it),
                       lambda e, par=par, xt=xt, lp=lp: e.activation(
                           outt2[:, par * D:(par + 1) * D], xt, AF.Square,
                           accum_out=lns[:, lp + 1:lp + 2]))
                pr.wait("v", ("x", it))
                pr.wait("v", ("std", it - 2), must=False)
                pr.inc("v", "V", ("lnr", it),
                       lambda e, xt=xt, lp=lp: e.reduce_sum(
                           lns[:, lp:lp + 1], xt, AX.X, negate=True))
                pr.wait("s", ("lnr", it))
                pr.raw("s", lambda e, lp=lp: e.activation(
                    lns[:, lp + 4:lp + 5], lns[:, lp:lp + 1], AF.Square,
                    scale=1.0 / D))
                pr.inc("s", "S", ("lnm", it),
                       lambda e, lp=lp: e.mul(
                           lns[:, lp + 2:lp + 3], lns[:, lp:lp + 1], 1.0 / D))
                pr.wait("v", ("lnm", it))
                pr.inc("v", "V", ("lnv", it),
                       lambda e, lp=lp: e.tensor_scalar(
                           lns[:, lp + 5:lp + 6], lns[:, lp + 1:lp + 2],
                           1.0 / D, lns[:, lp + 4:lp + 5],
                           ALU.mult, ALU.subtract))
                pr.wait("s", ("lnv", it))
                pr.inc("s", "S", ("std", it),
                       lambda e, lp=lp: e.activation(
                           lns[:, lp + 6:lp + 7], lns[:, lp + 5:lp + 6],
                           AF.Sqrt, bias=c_eps))
                pr.wait("v", ("std", it))
                pr.wait("v", ("tp", it - 2), must=False)
                pr.raw("v", lambda e, lp=lp: e.reciprocal(
                    lns[:, lp + 4:lp + 5], lns[:, lp + 6:lp + 7]))
                for _ in range(4):
                    pr.raw("v", lambda e: e.engine_nop())
                pr.inc("v", "V", ("xn", it),
                       lambda e, par=par, xt=xt, lp=lp: e.tensor_scalar(
                           xn2[:, par * D:(par + 1) * D], xt,
                           lns[:, lp + 2:lp + 3], lns[:, lp + 4:lp + 5],
                           ALU.add, ALU.mult))
                pr.wait("t", ("xn", it))
                pr.wait("t", ("tpe", it - 2), must=False)
                if pq is not None and g - g_lo < 2:
                    pr.wait("t", ("mte",) + pq + (1,), must=False)
                    pr.wait("t", ("mte",) + pq + (3,), must=False)
                pst = ps[6 + par][:].bitcast(BF16)
                for mi in range(NM):
                    key = ("tp", it) if mi == NM - 1 else None
                    fn = (lambda e, par=par, mi=mi, pst=pst: e.transpose(
                        pst[:, mi * 128:(mi + 1) * 128],
                        xn2[:, par * D + mi * 128:par * D + (mi + 1) * 128],
                        ident16[:]))
                    (pr.inc("t", "T", key, fn) if key else pr.raw("t", fn))
                j0 = (g - 4 * q) * 128 + 4
                jlo, jhi = max(j0, 0), min(j0 + 128, XW)
                pr.wait("s", ("tp", it))
                if pq is not None and g == g_lo:
                    pr.wait("s", ("lgm",) + pq + (1, 7), must=False)
                src = pst.rearrange("p (m c) -> p m c", m=8)[
                    :, :, jlo - j0:jhi - j0]
                dst = xnT[:].rearrange("p (m j) -> p m j", m=8)[:, :, jlo:jhi]
                pr.inc("s", "S", ("tpe", it),
                       lambda e, src=src, dst=dst: e.copy(dst, src))

            pieceit[pc] = it_ctr[0] - 1

        def mk_fe(pc, pq, pp):
            b, q = pc
            l0 = q * LQ
            start = l0 - 1 if q > 0 else 0
            end = min(l0 + 514, L)
            cnt = end - start
            jA = start - (l0 - 2)
            nb_ = cnt - 512
            moff = start - l0 + 4

            def fe_dc(i):
                dc = FE_ORDER[i]
                par = dc % 2
                psA, psB = ps[par * 2], ps[par * 2 + 1]
                if i < 2 and pq is not None:
                    pr.wait("g", ("xz",) + pq + (FE_ORDER[14 + i],),
                            must=False)
                if i >= 2:
                    pr.wait("g", ("xz", b, q, FE_ORDER[i - 2]))
                if (b, q) == (0, 0) and i == 0:
                    pr.wait("g", ("wcast", 0))
                wslot = (i % 2) * 1024
                pr.inc("g", "WG", ("winb", b, q, dc),
                       lambda e, dc=dc, wslot=wslot: e.dma_start(
                           Winb[:, wslot:wslot + 1024],
                           win16_h.ap()[dc * 128:(dc + 1) * 128, :]))
                if i == 0:
                    pr.wait("t", ("tpe", pieceit[pc]))
                    if pq is not None:
                        pr.wait("t", ("Cev",) + pq + (15,), must=False)
                if i < 2 and pq is not None:
                    pr.wait("t", ("xze",) + pq + (FE_ORDER[14 + i],),
                            must=False)
                if i >= 2:
                    pr.wait("t", ("xze", b, q, FE_ORDER[i - 2]))
                pr.wait("t", ("winb", b, q, dc))
                for mi in range(NM):
                    pr.raw("t", lambda e, mi=mi, psA=psA, moff=moff,
                           wslot=wslot:
                           e.matmul(psA[:],
                                    Winb[:, wslot + mi * 128:
                                         wslot + (mi + 1) * 128],
                                    xnT[:, mi * XW + moff:
                                        mi * XW + moff + 512],
                                    start=(mi == 0), stop=(mi == NM - 1)))
                for mi in range(NM):
                    key = ("xz", b, q, dc) if mi == NM - 1 else None
                    fn = (lambda e, mi=mi, psB=psB, moff=moff, nb_=nb_,
                          wslot=wslot:
                          e.matmul(psB[:, 0:nb_],
                                   Winb[:, wslot + mi * 128:
                                        wslot + (mi + 1) * 128],
                                   xnT[:, mi * XW + moff + 512:
                                       mi * XW + moff + 512 + nb_],
                                   start=(mi == 0), stop=(mi == NM - 1)))
                    (pr.inc("t", "T", key, fn) if key else pr.raw("t", fn))
                xzt = xz16[:, 0:516]
                pr.wait("s", ("xz", b, q, dc))
                if i < 1 and pq is not None:
                    pr.wait("s", ("conv",) + pq + (FE_ORDER[15],),
                            must=False)
                if i >= 1:
                    pr.wait("s", ("conv", b, q, FE_ORDER[i - 1]))
                if q == 0:
                    pr.raw("s", lambda e, xzt=xzt: e.memzero(xzt[:, 0:2]))
                if q == NQ - 1:
                    pr.raw("s", lambda e, xzt=xzt: e.memzero(xzt[:, 514:516]))
                pr.raw("s", lambda e, xzt=xzt, psA=psA, jA=jA:
                       e.copy(xzt[:, jA:jA + 512], psA[:]))
                pr.inc("s", "S", ("xze", b, q, dc),
                       lambda e, xzt=xzt, psB=psB, jA=jA, nb_=nb_:
                       e.copy(xzt[:, jA + 512:jA + 512 + nb_], psB[:, 0:nb_]))
                K = kx32 if dc < 8 else kz32
                kc = dc if dc < 8 else dc - 8
                dg = diag8[:]
                if (b, q) == (0, 0) and i == 0:
                    pr.wait("s", ("cst", 5))
                if i < 1 and pq is not None:
                    pr.wait("s", ("conv",) + pq + (FE_ORDER[15],),
                            must=False)
                if i >= 1:
                    pr.wait("s", ("conv", b, q, FE_ORDER[i - 1]))
                for j in range(4):
                    key = ("diag", b, q, dc) if j == 3 else None
                    fn = (lambda e, dg=dg, K=K, kc=kc, j=j: e.activation(
                        dg[:, j * 128:(j + 1) * 128], ident16[:], AF.Copy,
                        scale=K[:, kc * 4 + j:kc * 4 + j + 1]))
                    (pr.inc("s", "S", key, fn) if key else pr.raw("s", fn))
                cpar = i % 2
                pr.wait("t", ("xze", b, q, dc))
                pr.wait("t", ("diag", b, q, dc))
                if i < 2 and pq is not None:
                    pr.wait("t", ("gelu",) + pq + (FE_ORDER[14 + i],),
                            must=False)
                if i >= 2:
                    pr.wait("t", ("gelu", b, q, FE_ORDER[i - 2]))
                for j in range(4):
                    key = ("conv", b, q, dc) if j == 3 else None
                    fn = (lambda e, dg=dg, xzt=xzt, j=j, dpar=cpar:
                          e.matmul(ps[6 + dpar][:],
                                   dg[:, j * 128:(j + 1) * 128],
                                   xzt[:, j + 1:j + 513],
                                   start=(j == 0), stop=(j == 3)))
                    (pr.inc("t", "T", key, fn) if key else pr.raw("t", fn))

            def fe_gelu(i):
                dc = FE_ORDER[i]
                kc = dc if dc < 8 else dc - 8
                dpar = i % 2
                kpar = (b * NQ + q) % 2
                dst = (u16[:, (kpar * 8 + dc) * LQ:
                           (kpar * 8 + dc + 1) * LQ] if dc < 8 else
                       ycat16[:, (8 + kpar * 8 + dc - 8) * LQ:
                              (8 + kpar * 8 + dc - 7) * LQ])
                bvec = bx8 if dc < 8 else bz8
                if (b, q) == (0, 0) and dc == 0:
                    pr.wait("s", ("cst", 2))
                if i in (0, 8) and pp is not None:
                    pr.wait("s", ("odone",) + pp + (3,))
                pr.wait("s", ("conv", b, q, dc))
                pr.inc("s", "S", ("gelu", b, q, dc),
                       lambda e, dst=dst, bvec=bvec, kc=kc, dpar=dpar:
                       e.activation(dst, ps[6 + dpar][:], AF.Gelu,
                                    bias=bvec[:, kc:kc + 1]))

            return fe_dc, fe_gelu

        def emit_FE_all(pc, pq, pp):
            fe_dc, fe_gelu = mk_fe(pc, pq, pp)
            for i in range(NDC):
                fe_dc(i)
                fe_gelu(i)

        def emit_xproj(pc, pq):
            b, q = pc
            for c in range(NC):
                if (b, q) == (0, 0):
                    pr.wait("t", ("wxp", c))
                if c == 0 and pq is not None:
                    pr.wait("t", ("delta",) + pq + (6,), must=False)
                    pr.wait("t", ("xde",) + pq, must=False)
                pr.wait("t", ("gelu", b, q, c))
                kparx = (b * NQ + q) % 2
                key = ("xdbl", b, q) if c == NC - 1 else None
                fn = (lambda e, c=c, kparx=kparx: e.matmul(
                    ps[4][0:112, :], Wxp16[:, c * 112:(c + 1) * 112],
                    u16[:, (kparx * 8 + c) * LQ:(kparx * 8 + c + 1) * LQ],
                    start=(c == 0), stop=(c == NC - 1)))
                (pr.inc("t", "T", key, fn) if key else pr.raw("t", fn))
            pr.wait("s", ("xdbl", b, q))
            if pq is not None:
                pr.wait("s", ("dt",) + pq + (7,))
            pr.raw("s", lambda e: e.copy(dtv16[:], ps[4][0:64, :]))
            pr.raw("s", lambda e: e.copy(braw[:], ps[4][64:80, :]))
            pr.inc("s", "S", ("xde", b, q),
                   lambda e: e.copy(craw[:], ps[4][96:112, :]))

        def emit_F(pc, pq):
            b, q = pc
            l0 = q * LQ
            if (b, q) == (0, 0):
                pr.wait("t", ("bin",))
            if pq is not None:
                pr.wait("t", ("oev",) + pq + (3,))
            pr.wait("t", ("xze", b, q, 7))
            pr.wait("t", ("xde", b, q))
            for pi in range(2):
                ts_ = (2 * pi, 2 * pi + 1)
                if (b, q) == (0, 0) and pi == 0:
                    pr.wait("t", ("wfir", 7, 1))
                if pi == 1:
                    pr.wait("t", ("mte", b, q, 0))
                    pr.wait("t", ("mte", b, q, 1))
                for m in range(NM):
                    cnt_mm = 0
                    for h in range(2):
                        for t in ts_:
                            for wq in (2 * h, 2 * h + 1):
                                cnt_mm += 1
                                key = (("lgm", b, q, pi, m)
                                       if cnt_mm == 8 else None)
                                fn = (lambda e, t=t, wq=wq, m=m, h=h: e.matmul(
                                    ps[(t % 2) * 4 + wq][:],
                                    xnT[:, m * XW + 4 + t * 128:
                                        m * XW + 4 + t * 128 + 128],
                                    Wfi16r[:, m * 2048 + h * 1024 +
                                           (wq - 2 * h) * 512:
                                           m * 2048 + h * 1024 +
                                           (wq - 2 * h) * 512 + 512],
                                    start=(m == 0), stop=False))
                                (pr.inc("t", "T", key, fn) if key
                                 else pr.raw("t", fn))
                for t in ts_:
                    for wq in range(4):
                        key = ("lg", b, q, t) if wq == 3 else None
                        fn = (lambda e, t=t, wq=wq: e.matmul(
                            ps[(t % 2) * 4 + wq][:], ones16[0:1, :],
                            bin16[0:1, wq * 512:(wq + 1) * 512],
                            start=False, stop=True))
                        (pr.inc("t", "T", key, fn) if key
                         else pr.raw("t", fn))
                for t in ts_:
                    pr.wait("s", ("lg", b, q, t))
                    if t == 0 and pq is not None:
                        pr.wait("s", ("ycat",) + pq + (7,))
                    if (t > 0):
                        pr.wait("s", ("msk", b, q, t - 1))
                    for wq in range(4):
                        key = ("silu", b, q, t) if wq == 3 else None
                        fn = (lambda e, t=t, wq=wq: e.activation(
                            acts16[:, wq * 512:(wq + 1) * 512],
                            ps[(t % 2) * 4 + wq][:], AF.Silu))
                        (pr.inc("s", "S", key, fn) if key
                         else pr.raw("s", fn))
                    pr.wait("v", ("silu", b, q, t))
                    if (t > 0):
                        pr.wait("v", ("msk", b, q, t - 1))
                    pr.inc("v", "V", ("sgn", b, q, t),
                           lambda e: e.tensor_single_scalar(
                               sgn16[:], acts16, 0.0, ALU.is_gt))
                    dmv = dmap16[:, 0:2040].rearrange("p (t w) -> p t w", t=8)
                    sgv = sgn16[:, 0:2040].rearrange("p (t w) -> p t w", t=8)
                    pr.wait("g", ("sgn", b, q, t))
                    if t > 0:
                        pr.wait("g", ("msk", b, q, t - 1))
                    pr.raw("g", lambda e, dmv=dmv: e.memset(dmv[:, :, 0:1], 1.0))
                    for dl in range(7):
                        base = 2 ** dl - 1
                        p_sl = dmv[:, :, base:base + 2 ** dl]
                        s_sl = sgv[:, :, base:base + 2 ** dl]
                        nxt = dmv[:, :, 2 ** (dl + 1) - 1:
                                  2 ** (dl + 2) - 1].rearrange(
                            "p t (j two) -> p t j two", two=2)
                        odd = nxt[:, :, :, 1:2]
                        even = nxt[:, :, :, 0:1]
                        pr.raw("g", lambda e, odd=odd, p_sl=p_sl, s_sl=s_sl:
                               e.tensor_tensor(odd, p_sl, s_sl, ALU.mult))
                        fn = (lambda e, even=even, p_sl=p_sl, odd=odd:
                              e.tensor_sub(even, p_sl, odd))
                        if dl == 6:
                            pr.inc("g", "G", ("tree", b, q, t), fn)
                        else:
                            pr.raw("g", fn)
                    pr.wait("v", ("tree", b, q, t))
                    if t > 0:
                        pr.wait("v", ("mtp", b, q, t - 1))
                    pr.inc("v", "V", ("msk", b, q, t),
                           lambda e: e.tensor_tensor(msk16, acts16,
                                                     dmap16, ALU.mult))
                    pr.wait("t", ("msk", b, q, t))
                    pr.wait("t", ("mte", b, q, t - 2), must=False)
                    for wc in range(NWC):
                        pstt = ps[(t % 2) * 4 + wc // 4][:].bitcast(BF16)
                        key = ("mtp", b, q, t) if wc == NWC - 1 else None
                        fn = (lambda e, wc=wc, pstt=pstt: e.transpose(
                            pstt[:, (wc % 4) * 128:(wc % 4 + 1) * 128],
                            msk16[:, wc * 128:(wc + 1) * 128], ident16[:]))
                        (pr.inc("t", "T", key, fn) if key
                         else pr.raw("t", fn))
                    pr.wait("v", ("mtp", b, q, t))
                    if t == 0 and pq is not None:
                        pr.wait("v", ("odone",) + pq + (3,), must=False)
                    for k4 in range(4):
                        pstt = ps[(t % 2) * 4 + k4][:].bitcast(BF16)
                        src = pstt[:, 0:512].rearrange("p (w j) -> p w j", w=4)
                        dst = mskT16[:].rearrange("p (w j) -> p w j", w=16)[
                            :, 4 * k4:4 * k4 + 4, t * 128:(t + 1) * 128]
                        key = ("mte", b, q, t) if k4 == 3 else None
                        fn = (lambda e, src=src, dst=dst: e.tensor_copy(dst, src))
                        (pr.inc("v", "V", key, fn) if key else pr.raw("v", fn))


        def emit_S(pc, pq, pp, fe=None):
            b, q = pc
            l0 = q * LQ

            def splice(c):
                if fe is None:
                    return
                fe_dc, fe_gelu = fe
                if 1 <= c <= 6:
                    for i in (2 * c - 2, 2 * c - 1):
                        fe_dc(i)
                        fe_gelu(i)
                elif c == 7:
                    for i in (12, 13, 14, 15):
                        fe_dc(i)
                        fe_gelu(i)
            pr.wait("t", ("xde", b, q))
            pr.wait("t", ("silu", b, q, 3))
            pr.wait("t", ("mte", b, q, 3))
            for n in range(NST):
                pr.wait("t", ("Bev", b, q, n - 2), must=False)
                pr.inc("t", "T", ("repB", b, q, n),
                       lambda e, n=n: e.matmul(
                           ps[n % 2][:], onehot16[:, n * 128:(n + 1) * 128],
                           braw[:], start=True, stop=True))
                pr.wait("t", ("Cev", b, q, n - 2), must=False)
                pr.inc("t", "T", ("repC", b, q, n),
                       lambda e, n=n: e.matmul(
                           ps[2 + n % 2][:], onehot16[:, n * 128:(n + 1) * 128],
                           craw[:], start=True, stop=True))
                pr.wait("v", ("repB", b, q, n))
                pr.inc("v", "V", ("Bev", b, q, n),
                       lambda e, n=n: e.tensor_copy(
                           B16[:, n * LQ:(n + 1) * LQ], ps[n % 2][:]))
                pr.wait("v", ("repC", b, q, n))
                pr.inc("v", "V", ("Cev", b, q, n),
                       lambda e, n=n: e.tensor_copy(
                           C16[:, n * LQ:(n + 1) * LQ], ps[2 + n % 2][:]))
            if (b, q) == (0, 0):
                pr.wait("t", ("wdt",))

            def emit_delta(c2):
                dsl = (c2 % 2) * LQ
                if (b, q) == (0, 0) and c2 == 0:
                    pr.wait("s", ("cst", 0))
                pr.wait("s", ("dt", b, q, c2))
                if c2 >= 2:
                    pr.wait("s", ("du", b, q, c2 - 2), must=False)
                elif pq is not None:
                    pr.wait("s", ("du",) + pq + (6 + c2,), must=False)
                pr.raw("s", lambda e, c2=c2, dsl=dsl: e.activation(
                    delta2v[:, dsl:dsl + LQ], ps[4 + c2 % 2][:], AF.Exp,
                    bias=bdt8[:, c2:c2 + 1]))
                pr.inc("s", "S", ("delta", b, q, c2),
                       lambda e, dsl=dsl: e.activation(
                           delta2v[:, dsl:dsl + LQ], delta2v[:, dsl:dsl + LQ],
                           AF.Ln, bias=1.0))

            for c in range(NC):
                if c == 1 and pq is not None:
                    pr.wait("t", ("delta",) + pq + (7,), must=False)
                pr.wait("t", ("delta", b, q, c - 2), must=False)
                pr.inc("t", "T", ("dt", b, q, c),
                       lambda e, c=c: e.matmul(
                           ps[4 + c % 2][:], Wdt16[:, c * 128:(c + 1) * 128],
                           dtv16[:], start=True, stop=True))
                splice(c)
                dsl = (c % 2) * LQ
                if c == 0:
                    emit_delta(0)
                pr.wait("v", ("delta", b, q, c))
                if c == 0:
                    pr.wait("v", ("silu", b, q, 3))
                    pr.wait("v", ("tree", b, q, 3))
                    pr.wait("v", ("mtp", b, q, 3))
                kparu = (b * NQ + q) % 2
                pr.inc("v", "V", ("du", b, q, c),
                       lambda e, c=c, dsl=dsl, kparu=kparu: e.tensor_tensor(
                           du[:], delta2v[:, dsl:dsl + LQ],
                           u16[:, (kparu * 8 + c) * LQ:
                               (kparu * 8 + c + 1) * LQ],
                           ALU.mult))

                for n in range(NST):
                    idx = c * 16 + n
                    par = idx % 3
                    pidx = idx - 3
                    if pidx >= 0:
                        pr.wait("s", ("scan", b, q, pidx // 16, pidx % 16))
                    elif pq is not None:
                        pr.wait("s", ("scan",) + pq + (7, 13 + idx))
                    pr.inc("s", "S", ("dA", b, q, c, n),
                           lambda e, par=par, c=c, n=n, dsl=dsl: e.activation(
                               dA2[:, par * LQ:(par + 1) * LQ],
                               delta2v[:, dsl:dsl + LQ],
                               AF.Exp,
                               scale=a128[:, c * 16 + n:c * 16 + n + 1]))
                    if n == 1 and c + 1 < NC:
                        emit_delta(c + 1)
                    init = (0.0 if q == 0 else
                            hprev[:, c * 16 + n:c * 16 + n + 1])
                    if c == 0:
                        pr.wait("v", ("Bev", b, q, n))
                    pr.raw("v", lambda e, n=n: e.tensor_tensor(
                        dBu2[:], du[:],
                        B16[:, n * LQ:(n + 1) * LQ], ALU.mult))
                    pr.wait("v", ("dA", b, q, c, n))
                    pr.inc("v", "V", ("scan", b, q, c, n),
                           lambda e, par=par, n=n, init=init:
                           e.tensor_tensor_scan(
                               tmpall[:, n * LQ:(n + 1) * LQ],
                               dA2[:, par * LQ:(par + 1) * LQ],
                               dBu2[:],
                               init, ALU.mult, ALU.add))
                hsrc = tmpall[:].rearrange("p (n l) -> p n l", n=16)[
                    :, :, LQ - 1:LQ].squeeze(2)
                for _ in range(4):
                    pr.raw("v", lambda e: e.engine_nop())
                pr.raw("v", lambda e, hsrc=hsrc, c=c: e.tensor_copy(
                    hprev[:, c * 16:(c + 1) * 16], hsrc))
                if c == 0:
                    pr.wait("v", ("Cev", b, q, 15))
                pr.raw("v", lambda e: e.tensor_tensor(
                    tmpall[:], tmpall[:], C16[:], ALU.mult))
                for width in (4096, 2048, 1024):
                    pr.raw("v", lambda e, w=width: e.tensor_add(
                        tmpall[:, 0:w], tmpall[:, 0:w], tmpall[:, w:2 * w]))
                pr.raw("v", lambda e: e.tensor_add(
                    dBu2[:], tmpall[:, 0:512], tmpall[:, 512:1024]))
                if (b, q) == (0, 0) and c == 0:
                    pr.wait("v", ("cst", 1))
                if c == 0 and pp is not None:
                    pr.wait("v", ("odone",) + pp + (3,), must=False)
                kpar = (b * NQ + q) % 2
                if pq is not None:
                    pr.wait("v", ("mmo",) + pq + (c,), must=False)
                pr.inc("v", "V", ("ycat", b, q, c),
                       lambda e, c=c, kpar=kpar: e.scalar_tensor_tensor(
                           ycat16[:, c * LQ:(c + 1) * LQ],
                           u16[:, (kpar * 8 + c) * LQ:
                               (kpar * 8 + c + 1) * LQ],
                           d8[:, c:c + 1], dBu2[:], ALU.mult, ALU.add))
            splice(8)


        emit_P1(pieces[0], None)
        pr.wait("g", ("x", 3))
        pr.inc("g", "XG", ("wcast", 0),
               lambda e: e.dma_start(win16_h.ap(), win_h.ap()))
        pr.inc("g", "XG", ("wcast", 1),
               lambda e: e.dma_start(wo16_h.ap(), wout_h.ap()))
        pr.inc("g", "XG", ("wcast", 2),
               lambda e: e.dma_start(wfo16_h.ap(), wfoT_h.ap()))
        emit_FE_all(pieces[0], None, None)
        for kpc, pc in enumerate(pieces):
            pq = pieces[kpc - 1] if kpc >= 1 else None
            pp = pieces[kpc - 2] if kpc >= 2 else None
            emit_xproj(pc, pq)
            if pq is not None:
                emit_P9(pq, pp, nxt=pc)
            emit_F(pc, pq)
            fe_next = None
            if kpc + 1 < len(pieces):
                emit_P1(pieces[kpc + 1], pc)
                fe_next = mk_fe(pieces[kpc + 1], pc, pq)
            emit_S(pc, pq, pp, fe_next)
        emit_P9(pieces[-1], pieces[-2] if len(pieces) > 1 else None, nxt=None)

        pr.phase("fin")
        pr.wait_val("y", "O", pr.ctr["O"])
        if PHASE_LIMIT is not None:
            keep = []
            for name, st in pr.phases:
                if name != "fin" and name != "dbg":
                    keep.append((name, st))
                if name == PHASE_LIMIT:
                    break
            for name, st in pr.phases:
                if name == "dbg":
                    keep.append((name, st))
            octr = 0
            for _, st in keep:
                for eng in pr.ENG:
                    for it in st[eng]:
                        if it[0] == "inc" and it[4] == "O":
                            octr += it[3]
            fin = {k: [] for k in pr.ENG}
            fin["y"].append(("waitv", pr.sems["O"], octr))
            keep.append(("fin", fin))
            pr.phases = keep
        pr.realize()
    return nc


_NC_CACHE = []


def kernel(**inp):
    g = lambda k: np.ascontiguousarray(np.asarray(inp[k], dtype=np.float32))
    x = g("L0_attn_hidden_states")
    vec8 = lambda v: np.ascontiguousarray(v.reshape(8, 128).T)
    k32 = lambda K: np.ascontiguousarray(
        K.reshape(1024, 4).reshape(8, 128, 4).transpose(1, 0, 2).reshape(128, 32))

    wfiT = np.zeros((1024, WPAD), np.float32)
    wfiT[:, :2040] = g("L0_ffn_W_in").T
    wfoT = np.zeros((WPAD, 1024), np.float32)
    wfoT[:2040, :] = g("L0_ffn_W_out").T
    binp = np.zeros((1, WPAD), np.float32)
    binp[0, :2040] = g("L0_ffn_b_in")
    alog128 = np.ascontiguousarray(
        g("L0_attn_A_log").reshape(8, 128, 16).transpose(1, 0, 2).reshape(128, 128))

    wxp_re = np.zeros((1024, 112), np.float32)
    wxp96 = g("L0_attn_W_xproj")
    wxp_re[:, 0:80] = wxp96[:, 0:80]
    wxp_re[:, 96:112] = wxp96[:, 80:96]
    win_re = np.ascontiguousarray(
        g("L0_attn_W_in").reshape(8, 128, 16, 128).transpose(2, 1, 0, 3)
        .reshape(16 * 128, 1024))
    shared = dict(
        win_re=win_re, w_xp=np.ascontiguousarray(wxp_re),
        w_dt=g("L0_attn_W_dt"), w_out=g("L0_attn_W_out"),
        wfiT=np.ascontiguousarray(wfiT), wfoT=np.ascontiguousarray(wfoT),
        binp=binp, bdt8=vec8(g("L0_attn_b_dt")), d8=vec8(g("L0_attn_D")),
        bx8=vec8(g("L0_attn_b_x")), bz8=vec8(g("L0_attn_b_z")),
        kx32=k32(g("L0_attn_K_x")), kz32=k32(g("L0_attn_K_z")),
        alog=alog128,
    )
    xs = np.ascontiguousarray(x.reshape(8, NB * L, D))
    if not _NC_CACHE:
        _NC_CACHE.append(build_nc())
    nc = _NC_CACHE[0]
    in_maps = [dict(shared, x=xs[i]) for i in range(8)]
    res = run_bass_kernel_spmd(nc, in_maps, core_ids=list(range(8)))
    out = np.stack([r["out"] for r in res.results], axis=0)
    return out.reshape(32, L, D)




# version 25
# speedup vs baseline: 1.0062x; 1.0062x over previous
import numpy as np
from contextlib import ExitStack

import concourse.bass as bass
import concourse.mybir as mybir
from concourse.bass_utils import run_bass_kernel_spmd

F32 = mybir.dt.float32
BF16 = mybir.dt.bfloat16
AF = mybir.ActivationFunctionType
ALU = mybir.AluOpType
AX = mybir.AxisListType

PHASE_LIMIT = None
DEBUG_DUMP = False
LAST_PROG = None
P1_SERIAL = False
P8_STEPS = {"load", "mm", "bias", "silu", "sgn", "tree", "msk", "tp", "mte"}
LENIENT = False
P8_PAIRS = 2
NB = 4
L = 2048
D = 1024
LQ = 512
NQ = L // LQ
NM = 8
NDC = 16
NC = 8
NST = 16
NWC = 16
WPAD = 2048
XW = 520
FE_ORDER = list(range(8, 16)) + list(range(0, 8))


class Prog:
    ENG = ("y", "g", "t", "s", "v")

    def __init__(self, nc, ctx):
        self.nc = nc
        self.sems = {}
        self.ctr = {}
        for nm in ("T", "V", "S", "G", "X", "XG", "W", "WG", "O"):
            self.sems[nm] = ctx.enter_context(nc.semaphore("sem" + nm))
            self.ctr[nm] = 0
        self.tok = {}
        self.phases = []
        self.cur = None

    def phase(self, name):
        self.cur = {k: [] for k in self.ENG}
        self.phases.append((name, self.cur))

    def raw(self, eng, fn):
        self.cur[eng].append(("raw", fn))

    def inc(self, eng, ctr, key, fn):
        step = 16 if ctr in ("X", "XG", "W", "WG", "O") else 1
        self.ctr[ctr] += step
        if key is not None:
            assert key not in self.tok, key
            self.tok[key] = (ctr, self.ctr[ctr])
        self.cur[eng].append(("inc", fn, self.sems[ctr], step, ctr, key))

    def wait(self, eng, key, must=True):
        self.cur[eng].append(("wait", key, must))

    def wait_val(self, eng, ctr, val):
        self.cur[eng].append(("waitv", self.sems[ctr], val))

    def _remap_dma_bursts(self):
        for eng in self.ENG:
            pend = {}
            last = {}

            def flush():
                for ctr, keys in pend.items():
                    for k in keys:
                        self.tok[k] = (ctr, last[ctr])
                pend.clear()

            for _, streams in self.phases:
                for it in streams[eng]:
                    if it[0] == "inc" and it[4] in ("X", "XG", "W", "WG", "O"):
                        ctr, key = it[4], it[5]
                        last[ctr] = last.get(ctr, 0) + it[3]
                        if key is not None:
                            pend.setdefault(ctr, []).append(key)
                    elif it[0] == "waitv":
                        flush()
                    elif it[0] == "wait" and it[1] in self.tok:
                        flush()
                flush()

    def realize(self):
        self._remap_dma_bursts()
        nc = self.nc
        for name, streams in self.phases:
            with nc.Block(name) as blk:
                for eng in self.ENG:
                    items = streams[eng]
                    if not items:
                        continue
                    attr = {"y": "sync", "g": "gpsimd", "t": "tensor",
                            "s": "scalar", "v": "vector"}[eng]

                    def body(e, items=items):
                        for it in items:
                            kind = it[0]
                            if kind == "raw":
                                it[1](e)
                            elif kind == "inc":
                                fn, sem, step = it[1], it[2], it[3]
                                fn(e).then_inc(sem, step)
                            elif kind == "wait":
                                _, key, must = it
                                if key in self.tok:
                                    ctr, v = self.tok[key]
                                    e.wait_ge(self.sems[ctr], v)
                                elif must and not LENIENT:
                                    raise KeyError(f"missing token {key}")
                            else:
                                _, sem, v = it
                                e.wait_ge(sem, v)

                    getattr(blk, attr)(body)


def build_nc():
    global LAST_PROG
    nc = bass.Bass("TRN2", target_bir_lowering=False,
                   detect_race_conditions=False)

    x_h = nc.dram_tensor("x", [NB * L, D], F32, kind="ExternalInput")
    out_h = nc.dram_tensor("out", [NB * L, D], F32, kind="ExternalOutput")
    win_h = nc.dram_tensor("win_re", [16 * 128, 1024], F32,
                           kind="ExternalInput")
    wxp_h = nc.dram_tensor("w_xp", [1024, 112], F32, kind="ExternalInput")
    wdt_h = nc.dram_tensor("w_dt", [64, 1024], F32, kind="ExternalInput")
    wout_h = nc.dram_tensor("w_out", [2048, 1024], F32, kind="ExternalInput")
    wfiT_h = nc.dram_tensor("wfiT", [1024, WPAD], F32, kind="ExternalInput")
    wfoT_h = nc.dram_tensor("wfoT", [WPAD, 1024], F32, kind="ExternalInput")
    bin_h = nc.dram_tensor("binp", [1, WPAD], F32, kind="ExternalInput")
    bdt_h = nc.dram_tensor("bdt8", [128, 8], F32, kind="ExternalInput")
    d8_h = nc.dram_tensor("d8", [128, 8], F32, kind="ExternalInput")
    bx_h = nc.dram_tensor("bx8", [128, 8], F32, kind="ExternalInput")
    bz_h = nc.dram_tensor("bz8", [128, 8], F32, kind="ExternalInput")
    kx_h = nc.dram_tensor("kx32", [128, 32], F32, kind="ExternalInput")
    kz_h = nc.dram_tensor("kz32", [128, 32], F32, kind="ExternalInput")
    alog_h = nc.dram_tensor("alog", [128, 128], F32, kind="ExternalInput")
    win16_h = nc.dram_tensor("win16d", [16 * 128, 1024], BF16, kind="Internal")
    wfo16_h = nc.dram_tensor("wfo16d", [WPAD, 1024], BF16, kind="Internal")
    wo16_h = nc.dram_tensor("wo16d", [2048, 1024], BF16, kind="Internal")

    ctx = ExitStack()
    sb = lambda n, s, d=F32: ctx.enter_context(nc.sbuf_tensor(n, s, d))
    with ctx:
        Wfi16r = sb("Wfi16r", [128, 16 * 1024], BF16)
        Winb = sb("Winb", [128, 2 * 1024], BF16)
        Wxp16 = sb("Wxp16", [128, 8 * 112], BF16)
        Wdt16 = sb("Wdt16", [64, 1024], BF16)
        bin16 = sb("bin16", [1, WPAD], BF16)
        ones16 = sb("ones16", [1, 128], BF16)
        ident16 = sb("ident16", [128, 128], BF16)
        bdt8 = sb("bdt8s", [128, 8]); d8 = sb("d8s", [128, 8])
        bx8 = sb("bx8s", [128, 8]); bz8 = sb("bz8s", [128, 8])
        kx32 = sb("kx32s", [128, 32]); kz32 = sb("kz32s", [128, 32])
        a128 = sb("a128", [128, 128])
        hprev = sb("hprev", [128, 128])
        xnT = sb("xnT", [128, 8 * XW], BF16)
        u16 = sb("u16", [128, 2 * 8 * LQ], BF16)
        ycat16 = sb("ycat16", [128, 24 * LQ], BF16)
        B16 = sb("B16", [128, 16 * LQ], BF16)
        C16 = sb("C16", [128, 16 * LQ], BF16)
        tmpall = sb("tmpall", [128, 16 * LQ], BF16)
        mskT16 = sb("mskT16", [128, 16 * LQ], BF16)
        xz16 = sb("xz16", [128, 516 + 8], BF16)
        diag8 = sb("diag8", [128, 512], BF16)
        dtv16 = sb("dtv16", [64, LQ], BF16)
        braw = sb("braw", [16, LQ], BF16)
        craw = sb("craw", [16, LQ], BF16)
        delta2v = sb("delta2v", [128, 2 * LQ])
        du = sb("du", [128, LQ])
        dA2 = sb("dA2", [128, 3 * LQ])
        dBu2 = sb("dBu2", [128, LQ], BF16)
        xt2 = sb("xt2", [128, 2 * D], BF16)
        xt9 = sb("xt9", [128, 2 * D], BF16)
        xn2 = sb("xn2", [128, 2 * D], BF16)
        outt2 = sb("outt2", [128, 2 * D])
        Wo16 = sb("Wo16", [128, 2 * 1024], BF16)
        Wf16 = sb("Wf16", [128, 2 * 1024], BF16)
        lns = sb("lns", [128, 14])
        cpool = xz16[:, 516:524].bitcast(F32)
        c_eps = cpool[:, 0:1]
        c_invd = cpool[:, 1:2]
        nc.const_aps.register(F32, 1.0 / D, c_invd)
        onehot16 = sb("onehot16", [16, 16 * 128], BF16)

        ps = [ctx.enter_context(nc.psum_tensor(f"ps{i}", [128, 512], F32))
              for i in range(8)]

        acts16 = tmpall[:, 0:2048]
        sgn16 = tmpall[:, 2048:4096]
        dmap16 = tmpall[:, 4096:6144]
        msk16 = tmpall[:, 6144:8192]

        pr = Prog(nc, ctx)
        LAST_PROG = pr

        pr.phase("p0")
        for i, (dst, src) in enumerate([(bdt8, bdt_h), (d8, d8_h), (bx8, bx_h),
                                        (bz8, bz_h), (kx32, kx_h), (kz32, kz_h)]):
            pr.inc("y", "X", ("cst", i),
                   lambda e, d=dst, s=src: e.dma_start(d[:], s.ap()))
        pr.inc("y", "X", ("alog",), lambda e: e.dma_start(a128[:], alog_h.ap()))
        for m in range(NM):
            for h in range(2):
                pr.inc("g", "WG", ("wfir", m, h),
                       lambda e, m=m, h=h: e.dma_start(
                           Wfi16r[:, m * 2048 + h * 1024:
                                  m * 2048 + (h + 1) * 1024],
                           wfiT_h.ap()[m * 128:(m + 1) * 128,
                                       h * 1024:(h + 1) * 1024]))
        for c in range(NC):
            pr.inc("g", "WG", ("wxp", c),
                   lambda e, c=c: e.dma_start(
                       Wxp16[:, c * 112:(c + 1) * 112],
                       wxp_h.ap()[c * 128:(c + 1) * 128, :]))
        pr.inc("g", "WG", ("wdt",), lambda e: e.dma_start(Wdt16[:], wdt_h.ap()))
        pr.inc("g", "WG", ("bin",), lambda e: e.dma_start(bin16[:], bin_h.ap()))
        pr.inc("g", "G", ("iota",),
               lambda e: e.iota(xn2[:, 0:128], [[1, 128]],
                                channel_multiplier=-1,
                                allow_small_or_imprecise_dtypes=True))
        pr.inc("g", "G", ("ioh",),
               lambda e: e.iota(tmpall[0:16, 0:2048], [[1, 16], [0, 128]],
                                channel_multiplier=-1,
                                allow_small_or_imprecise_dtypes=True))
        pr.wait("s", ("alog",))
        pr.inc("s", "S", ("aexp",),
               lambda e: e.activation(outt2[:, 0:128], a128[:], AF.Exp))
        pr.wait("v", ("aexp",))
        pr.raw("v", lambda e: e.tensor_scalar_mul(a128[:], outt2[:, 0:128],
                                                  -1.0))
        pr.raw("v", lambda e: e.memset(c_eps, 1e-5))
        pr.raw("v", lambda e: e.memset(c_invd, 1.0 / D))
        pr.raw("v", lambda e: e.memset(xnT[:], 0.0))
        pr.raw("v", lambda e: e.memset(tmpall[:, 4096 + 2040:4096 + 2048],
                                       0.0))
        pr.wait("v", ("iota",))
        pr.raw("v", lambda e: e.tensor_single_scalar(ident16[:], xn2[:, 0:128],
                                                     0.0, ALU.is_equal))
        pr.raw("v", lambda e: e.memset(ones16[:], 1.0))
        pr.wait("v", ("ioh",))
        pr.raw("v", lambda e: e.tensor_single_scalar(
            onehot16[:], tmpall[0:16, 0:2048], 0.0, ALU.is_equal))

        last_out = [None, None]
        it_ctr = [0]

        pr.phase("main")
        pieces = [(b, q) for b in range(NB) for q in range(NQ)]

        def emit_P9(pk, pp, nxt=None):
            b, q = pk
            l0 = q * LQ
            def emit_x9(t):
                r0 = b * L + l0 + t * 128
                pr.wait("g", ("res", b, q, t - 2), must=False)
                if t < 2 and pp is not None:
                    pr.wait("g", ("res",) + pp + (2 + t,), must=False)
                pr.inc("g", "XG", ("x9", b, q, t),
                       lambda e, t=t, r0=r0: e.dma_start(
                           xt9[:, (t % 2) * D:(t % 2 + 1) * D],
                           x_h.ap()[r0:r0 + 128, :]))
            emit_x9(0)
            emit_x9(1)
            dcseq = list(range(8, 16)) + list(range(0, 8))

            def emit_wf(wc):
                if wc == 0 and pp is not None:
                    pr.wait("y", ("mmf",) + pp + (14,), must=False)
                if wc == 1 and pp is not None:
                    pr.wait("y", ("odone",) + pp + (3,), must=False)
                pr.wait("y", ("mmf", b, q, wc - 2), must=False)
                pr.inc("y", "W", ("wf", b, q, wc),
                       lambda e, wc=wc: e.dma_start(
                           Wf16[:, (wc % 2) * 1024:(wc % 2 + 1) * 1024],
                           wfo16_h.ap()[wc * 128:(wc + 1) * 128, :]))

            if (b, q) == (0, 0):
                pr.wait("y", ("wcast", 2))
            emit_wf(0)
            emit_wf(1)
            for i, dc in enumerate(dcseq):
                if i < 2 and pp is not None:
                    pr.wait("y", ("mmo",) + pp + (6 + i,), must=False)
                if i >= 2:
                    pr.wait("y", ("mmo", b, q, dcseq[i - 2]))
                pr.inc("y", "W", ("wo", b, q, dc),
                       lambda e, dc=dc: e.dma_start(
                           Wo16[:, (dc % 2) * 1024:(dc % 2 + 1) * 1024],
                           wo16_h.ap()[dc * 128:(dc + 1) * 128, :]))
            for wc in range(2, NWC):
                emit_wf(wc)

            for i, dc in enumerate(dcseq):
                pr.wait("t", ("wo", b, q, dc))
                if i == 0:
                    pr.wait("t", ("silu", b, q, 3))
                    pr.wait("t", ("mte", b, q, 3))
                    pr.wait("t", ("gelu", b, q, 15))
                    pr.wait("t", ("delta", b, q, 7))
                    pr.wait("t", ("Cev", b, q, 15))
                    if nxt is not None:
                        pr.wait("t", ("xze",) + nxt + (7,))
                        pr.wait("t", ("xde",) + nxt)
                if dc < 8:
                    pr.wait("t", ("ycat", b, q, dc))
                cnt_mm = 0
                for t in range(4):
                    for jh in range(2):
                        cnt_mm += 1
                        key = ("mmo", b, q, dc) if cnt_mm == 8 else None
                        kpar = (b * NQ + q) % 2
                        yoff = (dc * LQ if dc < 8 else
                                (8 + kpar * 8 + dc - 8) * LQ)
                        fn = (lambda e, dc=dc, t=t, jh=jh, yoff=yoff,
                              i=i: e.matmul(
                            ps[t * 2 + jh][:],
                            ycat16[:, yoff + t * 128:yoff + (t + 1) * 128],
                            Wo16[:, (dc % 2) * 1024 + jh * 512:
                                 (dc % 2) * 1024 + (jh + 1) * 512],
                            start=(i == 0), stop=False))
                        (pr.inc("t", "T", key, fn) if key
                         else pr.raw("t", fn))
            for wc in range(NWC):
                pr.wait("t", ("wf", b, q, wc))
                cnt_mm = 0
                for t in range(4):
                    for jh in range(2):
                        cnt_mm += 1
                        last_wc = (wc == NWC - 1)
                        if last_wc and jh == 1:
                            key = ("odone", b, q, t)
                        elif (not last_wc) and cnt_mm == 8:
                            key = ("mmf", b, q, wc)
                        else:
                            key = None
                        fn = (lambda e, wc=wc, t=t, jh=jh: e.matmul(
                            ps[t * 2 + jh][:],
                            mskT16[:, wc * LQ + t * 128:
                                   wc * LQ + (t + 1) * 128],
                            Wf16[:, (wc % 2) * 1024 + jh * 512:
                                 (wc % 2) * 1024 + (jh + 1) * 512],
                            start=False, stop=(wc == NWC - 1)))
                        (pr.inc("t", "T", key, fn) if key
                         else pr.raw("t", fn))
            for t in range(4):
                par = t % 2
                pr.wait("s", ("odone", b, q, t))
                if last_out[par] is not None:
                    pr.wait("s", last_out[par])
                pr.raw("s", lambda e, t=t, par=par: e.activation(
                    outt2[:, par * D:par * D + 512], ps[t * 2][:],
                    AF.Copy))
                pr.inc("s", "S", ("oev", b, q, t),
                       lambda e, t=t, par=par: e.activation(
                           outt2[:, par * D + 512:(par + 1) * D],
                           ps[t * 2 + 1][:], AF.Copy))
                pr.wait("v", ("oev", b, q, t))
                pr.wait("v", ("x9", b, q, t))
                pr.inc("v", "V", ("res", b, q, t),
                       lambda e, t=t, par=par: e.tensor_tensor(
                           outt2[:, par * D:(par + 1) * D],
                           outt2[:, par * D:(par + 1) * D],
                           xt9[:, par * D:(par + 1) * D], ALU.add))
                if t + 2 < 4:
                    emit_x9(t + 2)
                r0 = b * L + l0 + t * 128
                pr.wait("y", ("res", b, q, t))
                pr.inc("y", "O", ("out", b, q, t),
                       lambda e, par=par, r0=r0: e.dma_start(
                           out_h.ap()[r0:r0 + 128, :],
                           outt2[:, par * D:(par + 1) * D]))
                last_out[par] = ("out", b, q, t)

        pieceit = {}

        def emit_P1(pc, pq):
            b, q = pc
            l0 = q * LQ
            g_lo, g_hi = max(q * 4 - 1, 0), min(q * 4 + 4, 15)
            for g in range(g_lo, g_hi + 1):
                it = it_ctr[0]; it_ctr[0] += 1
                par = it % 2
                lp = par * 7
                r0 = b * L + g * 128
                xt = xt2[:, par * D:(par + 1) * D]
                pr.wait("g", ("xn", it - 2), must=False)
                pr.wait("g", ("ss", it - 2), must=False)
                pr.inc("g", "XG", ("x", it),
                       lambda e, xt=xt, r0=r0: e.dma_start(
                           xt, x_h.ap()[r0:r0 + 128, :]))
                if last_out[par] is not None:
                    pr.wait("s", last_out[par])
                pr.wait("s", ("x", it))
                pr.inc("s", "S", ("ss", it),
                       lambda e, par=par, xt=xt, lp=lp: e.activation(
                           outt2[:, par * D:(par + 1) * D], xt, AF.Square,
                           accum_out=lns[:, lp + 1:lp + 2]))
                pr.wait("v", ("x", it))
                pr.wait("v", ("std", it - 2), must=False)
                pr.inc("v", "V", ("lnr", it),
                       lambda e, xt=xt, lp=lp: e.reduce_sum(
                           lns[:, lp:lp + 1], xt, AX.X, negate=True))
                pr.wait("s", ("lnr", it))
                pr.raw("s", lambda e, lp=lp: e.activation(
                    lns[:, lp + 4:lp + 5], lns[:, lp:lp + 1], AF.Square,
                    scale=1.0 / D))
                pr.inc("s", "S", ("lnm", it),
                       lambda e, lp=lp: e.mul(
                           lns[:, lp + 2:lp + 3], lns[:, lp:lp + 1], 1.0 / D))
                pr.wait("v", ("lnm", it))
                pr.inc("v", "V", ("lnv", it),
                       lambda e, lp=lp: e.tensor_scalar(
                           lns[:, lp + 5:lp + 6], lns[:, lp + 1:lp + 2],
                           1.0 / D, lns[:, lp + 4:lp + 5],
                           ALU.mult, ALU.subtract))
                pr.wait("s", ("lnv", it))
                pr.inc("s", "S", ("std", it),
                       lambda e, lp=lp: e.activation(
                           lns[:, lp + 6:lp + 7], lns[:, lp + 5:lp + 6],
                           AF.Sqrt, bias=c_eps))
                pr.wait("v", ("std", it))
                pr.wait("v", ("tp", it - 2), must=False)
                pr.raw("v", lambda e, lp=lp: e.reciprocal(
                    lns[:, lp + 4:lp + 5], lns[:, lp + 6:lp + 7]))
                for _ in range(4):
                    pr.raw("v", lambda e: e.engine_nop())
                pr.inc("v", "V", ("xn", it),
                       lambda e, par=par, xt=xt, lp=lp: e.tensor_scalar(
                           xn2[:, par * D:(par + 1) * D], xt,
                           lns[:, lp + 2:lp + 3], lns[:, lp + 4:lp + 5],
                           ALU.add, ALU.mult))
                pr.wait("t", ("xn", it))
                pr.wait("t", ("tpe", it - 2), must=False)
                if pq is not None and g - g_lo < 2:
                    pr.wait("t", ("mte",) + pq + (1,), must=False)
                    pr.wait("t", ("mte",) + pq + (3,), must=False)
                pst = ps[6 + par][:].bitcast(BF16)
                for mi in range(NM):
                    key = ("tp", it) if mi == NM - 1 else None
                    fn = (lambda e, par=par, mi=mi, pst=pst: e.transpose(
                        pst[:, mi * 128:(mi + 1) * 128],
                        xn2[:, par * D + mi * 128:par * D + (mi + 1) * 128],
                        ident16[:]))
                    (pr.inc("t", "T", key, fn) if key else pr.raw("t", fn))
                j0 = (g - 4 * q) * 128 + 4
                jlo, jhi = max(j0, 0), min(j0 + 128, XW)
                pr.wait("s", ("tp", it))
                if pq is not None and g == g_lo:
                    pr.wait("s", ("lgm",) + pq + (1, 7), must=False)
                src = pst.rearrange("p (m c) -> p m c", m=8)[
                    :, :, jlo - j0:jhi - j0]
                dst = xnT[:].rearrange("p (m j) -> p m j", m=8)[:, :, jlo:jhi]
                pr.inc("s", "S", ("tpe", it),
                       lambda e, src=src, dst=dst: e.copy(dst, src))

            pieceit[pc] = it_ctr[0] - 1

        def mk_fe(pc, pq, pp):
            b, q = pc
            l0 = q * LQ
            start = l0 - 1 if q > 0 else 0
            end = min(l0 + 514, L)
            cnt = end - start
            jA = start - (l0 - 2)
            nb_ = cnt - 512
            moff = start - l0 + 4

            def fe_dc(i):
                dc = FE_ORDER[i]
                par = dc % 2
                psA, psB = ps[par * 2], ps[par * 2 + 1]
                if i < 2 and pq is not None:
                    pr.wait("g", ("xz",) + pq + (FE_ORDER[14 + i],),
                            must=False)
                if i >= 2:
                    pr.wait("g", ("xz", b, q, FE_ORDER[i - 2]))
                if (b, q) == (0, 0) and i == 0:
                    pr.wait("g", ("wcast", 0))
                wslot = (i % 2) * 1024
                pr.inc("g", "WG", ("winb", b, q, dc),
                       lambda e, dc=dc, wslot=wslot: e.dma_start(
                           Winb[:, wslot:wslot + 1024],
                           win16_h.ap()[dc * 128:(dc + 1) * 128, :]))
                if i == 0:
                    pr.wait("t", ("tpe", pieceit[pc]))
                    if pq is not None:
                        pr.wait("t", ("Cev",) + pq + (15,), must=False)
                if i < 2 and pq is not None:
                    pr.wait("t", ("xze",) + pq + (FE_ORDER[14 + i],),
                            must=False)
                if i >= 2:
                    pr.wait("t", ("xze", b, q, FE_ORDER[i - 2]))
                pr.wait("t", ("winb", b, q, dc))
                for mi in range(NM):
                    pr.raw("t", lambda e, mi=mi, psA=psA, moff=moff,
                           wslot=wslot:
                           e.matmul(psA[:],
                                    Winb[:, wslot + mi * 128:
                                         wslot + (mi + 1) * 128],
                                    xnT[:, mi * XW + moff:
                                        mi * XW + moff + 512],
                                    start=(mi == 0), stop=(mi == NM - 1)))
                for mi in range(NM):
                    key = ("xz", b, q, dc) if mi == NM - 1 else None
                    fn = (lambda e, mi=mi, psB=psB, moff=moff, nb_=nb_,
                          wslot=wslot:
                          e.matmul(psB[:, 0:nb_],
                                   Winb[:, wslot + mi * 128:
                                        wslot + (mi + 1) * 128],
                                   xnT[:, mi * XW + moff + 512:
                                       mi * XW + moff + 512 + nb_],
                                   start=(mi == 0), stop=(mi == NM - 1)))
                    (pr.inc("t", "T", key, fn) if key else pr.raw("t", fn))
                xzt = xz16[:, 0:516]
                pr.wait("s", ("xz", b, q, dc))
                if i < 1 and pq is not None:
                    pr.wait("s", ("conv",) + pq + (FE_ORDER[15],),
                            must=False)
                if i >= 1:
                    pr.wait("s", ("conv", b, q, FE_ORDER[i - 1]))
                if q == 0:
                    pr.raw("s", lambda e, xzt=xzt: e.memzero(xzt[:, 0:2]))
                if q == NQ - 1:
                    pr.raw("s", lambda e, xzt=xzt: e.memzero(xzt[:, 514:516]))
                pr.raw("s", lambda e, xzt=xzt, psA=psA, jA=jA:
                       e.copy(xzt[:, jA:jA + 512], psA[:]))
                pr.inc("s", "S", ("xze", b, q, dc),
                       lambda e, xzt=xzt, psB=psB, jA=jA, nb_=nb_:
                       e.copy(xzt[:, jA + 512:jA + 512 + nb_], psB[:, 0:nb_]))
                K = kx32 if dc < 8 else kz32
                kc = dc if dc < 8 else dc - 8
                dg = diag8[:]
                if (b, q) == (0, 0) and i == 0:
                    pr.wait("s", ("cst", 5))
                if i < 1 and pq is not None:
                    pr.wait("s", ("conv",) + pq + (FE_ORDER[15],),
                            must=False)
                if i >= 1:
                    pr.wait("s", ("conv", b, q, FE_ORDER[i - 1]))
                for j in range(4):
                    key = ("diag", b, q, dc) if j == 3 else None
                    fn = (lambda e, dg=dg, K=K, kc=kc, j=j: e.activation(
                        dg[:, j * 128:(j + 1) * 128], ident16[:], AF.Copy,
                        scale=K[:, kc * 4 + j:kc * 4 + j + 1]))
                    (pr.inc("s", "S", key, fn) if key else pr.raw("s", fn))
                cpar = i % 2
                pr.wait("t", ("xze", b, q, dc))
                pr.wait("t", ("diag", b, q, dc))
                if i < 2 and pq is not None:
                    pr.wait("t", ("gelu",) + pq + (FE_ORDER[14 + i],),
                            must=False)
                if i >= 2:
                    pr.wait("t", ("gelu", b, q, FE_ORDER[i - 2]))
                for j in range(4):
                    key = ("conv", b, q, dc) if j == 3 else None
                    fn = (lambda e, dg=dg, xzt=xzt, j=j, dpar=cpar:
                          e.matmul(ps[6 + dpar][:],
                                   dg[:, j * 128:(j + 1) * 128],
                                   xzt[:, j + 1:j + 513],
                                   start=(j == 0), stop=(j == 3)))
                    (pr.inc("t", "T", key, fn) if key else pr.raw("t", fn))

            def fe_gelu(i):
                dc = FE_ORDER[i]
                kc = dc if dc < 8 else dc - 8
                dpar = i % 2
                kpar = (b * NQ + q) % 2
                dst = (u16[:, (kpar * 8 + dc) * LQ:
                           (kpar * 8 + dc + 1) * LQ] if dc < 8 else
                       ycat16[:, (8 + kpar * 8 + dc - 8) * LQ:
                              (8 + kpar * 8 + dc - 7) * LQ])
                bvec = bx8 if dc < 8 else bz8
                if (b, q) == (0, 0) and dc == 0:
                    pr.wait("s", ("cst", 2))
                if i in (0, 8) and pp is not None:
                    pr.wait("s", ("odone",) + pp + (3,))
                pr.wait("s", ("conv", b, q, dc))
                pr.inc("s", "S", ("gelu", b, q, dc),
                       lambda e, dst=dst, bvec=bvec, kc=kc, dpar=dpar:
                       e.activation(dst, ps[6 + dpar][:], AF.Gelu,
                                    bias=bvec[:, kc:kc + 1]))

            return fe_dc, fe_gelu

        def emit_FE_all(pc, pq, pp):
            fe_dc, fe_gelu = mk_fe(pc, pq, pp)
            for i in range(NDC):
                fe_dc(i)
                fe_gelu(i)

        def emit_xproj(pc, pq):
            b, q = pc
            for c in range(NC):
                if (b, q) == (0, 0):
                    pr.wait("t", ("wxp", c))
                if c == 0 and pq is not None:
                    pr.wait("t", ("delta",) + pq + (6,), must=False)
                    pr.wait("t", ("xde",) + pq, must=False)
                pr.wait("t", ("gelu", b, q, c))
                kparx = (b * NQ + q) % 2
                key = ("xdbl", b, q) if c == NC - 1 else None
                fn = (lambda e, c=c, kparx=kparx: e.matmul(
                    ps[4][0:112, :], Wxp16[:, c * 112:(c + 1) * 112],
                    u16[:, (kparx * 8 + c) * LQ:(kparx * 8 + c + 1) * LQ],
                    start=(c == 0), stop=(c == NC - 1)))
                (pr.inc("t", "T", key, fn) if key else pr.raw("t", fn))
            pr.wait("s", ("xdbl", b, q))
            if pq is not None:
                pr.wait("s", ("dt",) + pq + (7,))
            pr.raw("s", lambda e: e.copy(dtv16[:], ps[4][0:64, :]))
            pr.raw("s", lambda e: e.copy(braw[:], ps[4][64:80, :]))
            pr.inc("s", "S", ("xde", b, q),
                   lambda e: e.copy(craw[:], ps[4][96:112, :]))

        def emit_F(pc, pq):
            b, q = pc
            l0 = q * LQ
            if (b, q) == (0, 0):
                pr.wait("t", ("bin",))
            if pq is not None:
                pr.wait("t", ("oev",) + pq + (3,))
            pr.wait("t", ("xze", b, q, 7))
            pr.wait("t", ("xde", b, q))
            for pi in range(2):
                ts_ = (2 * pi, 2 * pi + 1)
                if (b, q) == (0, 0) and pi == 0:
                    pr.wait("t", ("wfir", 7, 1))
                if pi == 1:
                    pr.wait("t", ("mte", b, q, 0))
                    pr.wait("t", ("mte", b, q, 1))
                for m in range(NM):
                    cnt_mm = 0
                    for h in range(2):
                        for t in ts_:
                            for wq in (2 * h, 2 * h + 1):
                                cnt_mm += 1
                                key = (("lgm", b, q, pi, m)
                                       if cnt_mm == 8 else None)
                                fn = (lambda e, t=t, wq=wq, m=m, h=h: e.matmul(
                                    ps[(t % 2) * 4 + wq][:],
                                    xnT[:, m * XW + 4 + t * 128:
                                        m * XW + 4 + t * 128 + 128],
                                    Wfi16r[:, m * 2048 + h * 1024 +
                                           (wq - 2 * h) * 512:
                                           m * 2048 + h * 1024 +
                                           (wq - 2 * h) * 512 + 512],
                                    start=(m == 0), stop=False))
                                (pr.inc("t", "T", key, fn) if key
                                 else pr.raw("t", fn))
                for t in ts_:
                    for wq in range(4):
                        key = ("lg", b, q, t) if wq == 3 else None
                        fn = (lambda e, t=t, wq=wq: e.matmul(
                            ps[(t % 2) * 4 + wq][:], ones16[0:1, :],
                            bin16[0:1, wq * 512:(wq + 1) * 512],
                            start=False, stop=True))
                        (pr.inc("t", "T", key, fn) if key
                         else pr.raw("t", fn))
                for t in ts_:
                    pr.wait("s", ("lg", b, q, t))
                    if t == 0 and pq is not None:
                        pr.wait("s", ("ycat",) + pq + (7,))
                    if (t > 0):
                        pr.wait("s", ("msk", b, q, t - 1))
                    for wq in range(4):
                        key = ("silu", b, q, t) if wq == 3 else None
                        fn = (lambda e, t=t, wq=wq: e.activation(
                            acts16[:, wq * 512:(wq + 1) * 512],
                            ps[(t % 2) * 4 + wq][:], AF.Silu))
                        (pr.inc("s", "S", key, fn) if key
                         else pr.raw("s", fn))
                    pr.wait("v", ("silu", b, q, t))
                    if (t > 0):
                        pr.wait("v", ("msk", b, q, t - 1))
                    pr.inc("v", "V", ("sgn", b, q, t),
                           lambda e: e.tensor_single_scalar(
                               sgn16[:], acts16, 0.0, ALU.is_gt))
                    dmv = dmap16[:, 0:2040].rearrange("p (t w) -> p t w", t=8)
                    sgv = sgn16[:, 0:2040].rearrange("p (t w) -> p t w", t=8)
                    pr.wait("g", ("sgn", b, q, t))
                    if t > 0:
                        pr.wait("g", ("msk", b, q, t - 1))
                    pr.raw("g", lambda e, dmv=dmv: e.memset(dmv[:, :, 0:1], 1.0))
                    for dl in range(7):
                        base = 2 ** dl - 1
                        p_sl = dmv[:, :, base:base + 2 ** dl]
                        s_sl = sgv[:, :, base:base + 2 ** dl]
                        nxt = dmv[:, :, 2 ** (dl + 1) - 1:
                                  2 ** (dl + 2) - 1].rearrange(
                            "p t (j two) -> p t j two", two=2)
                        odd = nxt[:, :, :, 1:2]
                        even = nxt[:, :, :, 0:1]
                        pr.raw("g", lambda e, odd=odd, p_sl=p_sl, s_sl=s_sl:
                               e.tensor_tensor(odd, p_sl, s_sl, ALU.mult))
                        fn = (lambda e, even=even, p_sl=p_sl, odd=odd:
                              e.tensor_sub(even, p_sl, odd))
                        if dl == 6:
                            pr.inc("g", "G", ("tree", b, q, t), fn)
                        else:
                            pr.raw("g", fn)
                    pr.wait("v", ("tree", b, q, t))
                    if t > 0:
                        pr.wait("v", ("mtp", b, q, t - 1))
                    pr.inc("v", "V", ("msk", b, q, t),
                           lambda e: e.tensor_tensor(msk16, acts16,
                                                     dmap16, ALU.mult))
                    pr.wait("t", ("msk", b, q, t))
                    pr.wait("t", ("mte", b, q, t - 2), must=False)
                    for wc in range(NWC):
                        pstt = ps[(t % 2) * 4 + wc // 4][:].bitcast(BF16)
                        key = ("mtp", b, q, t) if wc == NWC - 1 else None
                        fn = (lambda e, wc=wc, pstt=pstt: e.transpose(
                            pstt[:, (wc % 4) * 128:(wc % 4 + 1) * 128],
                            msk16[:, wc * 128:(wc + 1) * 128], ident16[:]))
                        (pr.inc("t", "T", key, fn) if key
                         else pr.raw("t", fn))
                    pr.wait("v", ("mtp", b, q, t))
                    if t == 0 and pq is not None:
                        pr.wait("v", ("odone",) + pq + (3,), must=False)
                    for k4 in range(4):
                        pstt = ps[(t % 2) * 4 + k4][:].bitcast(BF16)
                        src = pstt[:, 0:512].rearrange("p (w j) -> p w j", w=4)
                        dst = mskT16[:].rearrange("p (w j) -> p w j", w=16)[
                            :, 4 * k4:4 * k4 + 4, t * 128:(t + 1) * 128]
                        key = ("mte", b, q, t) if k4 == 3 else None
                        fn = (lambda e, src=src, dst=dst: e.tensor_copy(dst, src))
                        (pr.inc("v", "V", key, fn) if key else pr.raw("v", fn))


        def emit_S(pc, pq, pp, fe=None):
            b, q = pc
            l0 = q * LQ

            def splice(c):
                if fe is None:
                    return
                fe_dc, fe_gelu = fe
                if 1 <= c <= 8:
                    for i in (2 * c - 2, 2 * c - 1):
                        fe_dc(i)
                        fe_gelu(i)
            pr.wait("t", ("xde", b, q))
            pr.wait("t", ("silu", b, q, 3))
            pr.wait("t", ("mte", b, q, 3))
            for n in range(NST):
                pr.wait("t", ("Bev", b, q, n - 2), must=False)
                pr.inc("t", "T", ("repB", b, q, n),
                       lambda e, n=n: e.matmul(
                           ps[n % 2][:], onehot16[:, n * 128:(n + 1) * 128],
                           braw[:], start=True, stop=True))
                pr.wait("t", ("Cev", b, q, n - 2), must=False)
                pr.inc("t", "T", ("repC", b, q, n),
                       lambda e, n=n: e.matmul(
                           ps[2 + n % 2][:], onehot16[:, n * 128:(n + 1) * 128],
                           craw[:], start=True, stop=True))
                pr.wait("v", ("repB", b, q, n))
                pr.inc("v", "V", ("Bev", b, q, n),
                       lambda e, n=n: e.tensor_copy(
                           B16[:, n * LQ:(n + 1) * LQ], ps[n % 2][:]))
                pr.wait("v", ("repC", b, q, n))
                pr.inc("v", "V", ("Cev", b, q, n),
                       lambda e, n=n: e.tensor_copy(
                           C16[:, n * LQ:(n + 1) * LQ], ps[2 + n % 2][:]))
            if (b, q) == (0, 0):
                pr.wait("t", ("wdt",))

            def emit_delta(c2):
                dsl = (c2 % 2) * LQ
                if (b, q) == (0, 0) and c2 == 0:
                    pr.wait("s", ("cst", 0))
                pr.wait("s", ("dt", b, q, c2))
                if c2 >= 2:
                    pr.wait("s", ("du", b, q, c2 - 2), must=False)
                elif pq is not None:
                    pr.wait("s", ("du",) + pq + (6 + c2,), must=False)
                pr.raw("s", lambda e, c2=c2, dsl=dsl: e.activation(
                    delta2v[:, dsl:dsl + LQ], ps[4 + c2 % 2][:], AF.Exp,
                    bias=bdt8[:, c2:c2 + 1]))
                pr.inc("s", "S", ("delta", b, q, c2),
                       lambda e, dsl=dsl: e.activation(
                           delta2v[:, dsl:dsl + LQ], delta2v[:, dsl:dsl + LQ],
                           AF.Ln, bias=1.0))

            for c in range(NC):
                if c == 1 and pq is not None:
                    pr.wait("t", ("delta",) + pq + (7,), must=False)
                pr.wait("t", ("delta", b, q, c - 2), must=False)
                pr.inc("t", "T", ("dt", b, q, c),
                       lambda e, c=c: e.matmul(
                           ps[4 + c % 2][:], Wdt16[:, c * 128:(c + 1) * 128],
                           dtv16[:], start=True, stop=True))
                splice(c)
                dsl = (c % 2) * LQ
                if c == 0:
                    emit_delta(0)
                pr.wait("v", ("delta", b, q, c))
                if c == 0:
                    pr.wait("v", ("silu", b, q, 3))
                    pr.wait("v", ("tree", b, q, 3))
                    pr.wait("v", ("mtp", b, q, 3))
                kparu = (b * NQ + q) % 2
                pr.inc("v", "V", ("du", b, q, c),
                       lambda e, c=c, dsl=dsl, kparu=kparu: e.tensor_tensor(
                           du[:], delta2v[:, dsl:dsl + LQ],
                           u16[:, (kparu * 8 + c) * LQ:
                               (kparu * 8 + c + 1) * LQ],
                           ALU.mult))

                for n in range(NST):
                    idx = c * 16 + n
                    par = idx % 3
                    pidx = idx - 3
                    if pidx >= 0:
                        pr.wait("s", ("scan", b, q, pidx // 16, pidx % 16))
                    elif pq is not None:
                        pr.wait("s", ("scan",) + pq + (7, 13 + idx))
                    pr.inc("s", "S", ("dA", b, q, c, n),
                           lambda e, par=par, c=c, n=n, dsl=dsl: e.activation(
                               dA2[:, par * LQ:(par + 1) * LQ],
                               delta2v[:, dsl:dsl + LQ],
                               AF.Exp,
                               scale=a128[:, c * 16 + n:c * 16 + n + 1]))
                    if n == 1 and c + 1 < NC:
                        emit_delta(c + 1)
                    init = (0.0 if q == 0 else
                            hprev[:, c * 16 + n:c * 16 + n + 1])
                    if c == 0:
                        pr.wait("v", ("Bev", b, q, n))
                    pr.raw("v", lambda e, n=n: e.tensor_tensor(
                        dBu2[:], du[:],
                        B16[:, n * LQ:(n + 1) * LQ], ALU.mult))
                    pr.wait("v", ("dA", b, q, c, n))
                    pr.inc("v", "V", ("scan", b, q, c, n),
                           lambda e, par=par, n=n, init=init:
                           e.tensor_tensor_scan(
                               tmpall[:, n * LQ:(n + 1) * LQ],
                               dA2[:, par * LQ:(par + 1) * LQ],
                               dBu2[:],
                               init, ALU.mult, ALU.add))
                hsrc = tmpall[:].rearrange("p (n l) -> p n l", n=16)[
                    :, :, LQ - 1:LQ].squeeze(2)
                for _ in range(4):
                    pr.raw("v", lambda e: e.engine_nop())
                pr.raw("v", lambda e, hsrc=hsrc, c=c: e.tensor_copy(
                    hprev[:, c * 16:(c + 1) * 16], hsrc))
                if c == 0:
                    pr.wait("v", ("Cev", b, q, 15))
                pr.raw("v", lambda e: e.tensor_tensor(
                    tmpall[:], tmpall[:], C16[:], ALU.mult))
                for width in (4096, 2048, 1024):
                    pr.raw("v", lambda e, w=width: e.tensor_add(
                        tmpall[:, 0:w], tmpall[:, 0:w], tmpall[:, w:2 * w]))
                pr.raw("v", lambda e: e.tensor_add(
                    dBu2[:], tmpall[:, 0:512], tmpall[:, 512:1024]))
                if (b, q) == (0, 0) and c == 0:
                    pr.wait("v", ("cst", 1))
                if c == 0 and pp is not None:
                    pr.wait("v", ("odone",) + pp + (3,), must=False)
                kpar = (b * NQ + q) % 2
                if pq is not None:
                    pr.wait("v", ("mmo",) + pq + (c,), must=False)
                pr.inc("v", "V", ("ycat", b, q, c),
                       lambda e, c=c, kpar=kpar: e.scalar_tensor_tensor(
                           ycat16[:, c * LQ:(c + 1) * LQ],
                           u16[:, (kpar * 8 + c) * LQ:
                               (kpar * 8 + c + 1) * LQ],
                           d8[:, c:c + 1], dBu2[:], ALU.mult, ALU.add))
            splice(8)


        emit_P1(pieces[0], None)
        pr.wait("g", ("x", 3))
        pr.inc("g", "XG", ("wcast", 0),
               lambda e: e.dma_start(win16_h.ap(), win_h.ap()))
        pr.inc("g", "XG", ("wcast", 1),
               lambda e: e.dma_start(wo16_h.ap(), wout_h.ap()))
        pr.inc("g", "XG", ("wcast", 2),
               lambda e: e.dma_start(wfo16_h.ap(), wfoT_h.ap()))
        emit_FE_all(pieces[0], None, None)
        for kpc, pc in enumerate(pieces):
            pq = pieces[kpc - 1] if kpc >= 1 else None
            pp = pieces[kpc - 2] if kpc >= 2 else None
            emit_xproj(pc, pq)
            if pq is not None:
                emit_P9(pq, pp, nxt=pc)
            emit_F(pc, pq)
            fe_next = None
            if kpc + 1 < len(pieces):
                emit_P1(pieces[kpc + 1], pc)
                fe_next = mk_fe(pieces[kpc + 1], pc, pq)
            emit_S(pc, pq, pp, fe_next)
        emit_P9(pieces[-1], pieces[-2] if len(pieces) > 1 else None, nxt=None)

        pr.phase("fin")
        pr.wait_val("y", "O", pr.ctr["O"])
        if PHASE_LIMIT is not None:
            keep = []
            for name, st in pr.phases:
                if name != "fin" and name != "dbg":
                    keep.append((name, st))
                if name == PHASE_LIMIT:
                    break
            for name, st in pr.phases:
                if name == "dbg":
                    keep.append((name, st))
            octr = 0
            for _, st in keep:
                for eng in pr.ENG:
                    for it in st[eng]:
                        if it[0] == "inc" and it[4] == "O":
                            octr += it[3]
            fin = {k: [] for k in pr.ENG}
            fin["y"].append(("waitv", pr.sems["O"], octr))
            keep.append(("fin", fin))
            pr.phases = keep
        pr.realize()
    return nc


_NC_CACHE = []


def kernel(**inp):
    g = lambda k: np.ascontiguousarray(np.asarray(inp[k], dtype=np.float32))
    x = g("L0_attn_hidden_states")
    vec8 = lambda v: np.ascontiguousarray(v.reshape(8, 128).T)
    k32 = lambda K: np.ascontiguousarray(
        K.reshape(1024, 4).reshape(8, 128, 4).transpose(1, 0, 2).reshape(128, 32))

    wfiT = np.zeros((1024, WPAD), np.float32)
    wfiT[:, :2040] = g("L0_ffn_W_in").T
    wfoT = np.zeros((WPAD, 1024), np.float32)
    wfoT[:2040, :] = g("L0_ffn_W_out").T
    binp = np.zeros((1, WPAD), np.float32)
    binp[0, :2040] = g("L0_ffn_b_in")
    alog128 = np.ascontiguousarray(
        g("L0_attn_A_log").reshape(8, 128, 16).transpose(1, 0, 2).reshape(128, 128))

    wxp_re = np.zeros((1024, 112), np.float32)
    wxp96 = g("L0_attn_W_xproj")
    wxp_re[:, 0:80] = wxp96[:, 0:80]
    wxp_re[:, 96:112] = wxp96[:, 80:96]
    win_re = np.ascontiguousarray(
        g("L0_attn_W_in").reshape(8, 128, 16, 128).transpose(2, 1, 0, 3)
        .reshape(16 * 128, 1024))
    shared = dict(
        win_re=win_re, w_xp=np.ascontiguousarray(wxp_re),
        w_dt=g("L0_attn_W_dt"), w_out=g("L0_attn_W_out"),
        wfiT=np.ascontiguousarray(wfiT), wfoT=np.ascontiguousarray(wfoT),
        binp=binp, bdt8=vec8(g("L0_attn_b_dt")), d8=vec8(g("L0_attn_D")),
        bx8=vec8(g("L0_attn_b_x")), bz8=vec8(g("L0_attn_b_z")),
        kx32=k32(g("L0_attn_K_x")), kz32=k32(g("L0_attn_K_z")),
        alog=alog128,
    )
    xs = np.ascontiguousarray(x.reshape(8, NB * L, D))
    if not _NC_CACHE:
        _NC_CACHE.append(build_nc())
    nc = _NC_CACHE[0]
    in_maps = [dict(shared, x=xs[i]) for i in range(8)]
    res = run_bass_kernel_spmd(nc, in_maps, core_ids=list(range(8)))
    out = np.stack([r["out"] for r in res.results], axis=0)
    return out.reshape(32, L, D)


